# version 04
# baseline (speedup 1.0000x reference)
import numpy as np
import ml_dtypes
from contextlib import ExitStack

import concourse.bass as bass
import concourse.mybir as mybir
from concourse.bass_utils import run_bass_kernel_spmd

BF = ml_dtypes.bfloat16
F32 = mybir.dt.float32
BF16 = mybir.dt.bfloat16
AF = mybir.ActivationFunctionType
ALU = mybir.AluOpType
AXX = mybir.AxisListType.X

L = 4096
N1, N2F, K1N, K2N = 32, 128, 128, 32
T = 128
NT = 8
TOK = 1024
NQ = 32
IB = 86
HS = 8
EPS = 1e-5

_CACHE = {}


def _tables():
    n1 = np.arange(N1); n2 = np.arange(N2F); k1 = np.arange(K1N); k2 = np.arange(K2N)
    ph1 = -2 * np.pi * (np.einsum('a,c->ac', n2, k1) / 128.0)[None, :, :] \
          - 2 * np.pi * (np.einsum('b,c->bc', n1, k1) / 4096.0)[:, None, :]
    W1re = np.cos(ph1); W1im = np.sin(ph1)
    ph2 = 2 * np.pi * np.einsum('b,d->bd', n1, k2) / 32.0
    C2 = np.cos(ph2); S2 = np.sin(ph2)
    phB = 2 * np.pi * (np.einsum('c,b->bc', k1, n1) / 4096.0)[:, :, None] \
          + 2 * np.pi * (np.einsum('c,a->ca', k1, n2) / 128.0)[None, :, :]
    WBc = np.cos(phB) / L; WBs = np.sin(phB) / L

    w1 = np.zeros((N2F, 2, N1, K1N))
    w1[:, 0] = W1re.transpose(1, 0, 2); w1[:, 1] = W1im.transpose(1, 0, 2)
    c_w1 = w1.reshape(N2F, 2 * N1 * K1N).astype(BF)

    w2 = np.zeros((2, N1, 2, K2N))
    w2[0, :, 0] = C2;  w2[1, :, 0] = S2
    w2[0, :, 1] = -S2; w2[1, :, 1] = C2
    c_w2 = w2.reshape(64, 64).astype(BF)

    wa = np.zeros((K2N, 2, N1)); wa[:, 0] = C2.T; wa[:, 1] = S2.T
    c_wa = np.tile(wa.reshape(K2N, 64), (4, 1)).astype(BF)

    wb = np.zeros((K1N, 2, N1, N2F))
    wb[:, 0] = WBc.transpose(1, 0, 2); wb[:, 1] = -WBs.transpose(1, 0, 2)
    c_wb = wb.reshape(K1N, 2 * N1 * N2F).astype(BF)

    wa2 = np.zeros((2, K2N, 2, N1))
    wa2[0, :, 0] = C2.T; wa2[1, :, 0] = -S2.T
    wa2[0, :, 1] = S2.T; wa2[1, :, 1] = C2.T
    c_wa2 = np.tile(wa2.reshape(64, 64), (2, 1)).astype(BF)

    c_id = np.eye(128, dtype=BF)
    return c_w1, c_w2, c_wa, c_wb, c_wa2, c_id


class Sem:
    def __init__(self, ctx, nc, name):
        self.h = ctx.enter_context(nc.semaphore(name))
        self.n = 0


def build_program(debug=False, nt=NT, do_ffn=True, astop=99):
    key = (bool(debug), nt, bool(do_ffn), astop)
    if key in _CACHE:
        return _CACHE[key]
    nc = bass.Bass("TRN2", target_bir_lowering=False, detect_race_conditions=False)
    ctx = ExitStack()

    x_in = nc.dram_tensor("x", [TOK, L], F32, kind="ExternalInput").ap()
    wg_in = nc.dram_tensor("wg", [IB, NQ, 128, 128], F32, kind="ExternalInput").ap()
    wu_in = nc.dram_tensor("wu", [IB, NQ, 128, 128], F32, kind="ExternalInput").ap()
    wd_in = nc.dram_tensor("wd", [HS, IB, 128, 512], F32, kind="ExternalInput").ap()
    out_t = nc.dram_tensor("out", [TOK, L], F32, kind="ExternalOutput").ap()

    KIND = "ExternalOutput" if debug else "Internal"
    d_x1 = nc.dram_tensor("d_x1", [TOK, L], F32, kind=KIND).ap()
    d_v2 = nc.dram_tensor("d_v2", [NQ, 128, TOK], BF16, kind=KIND).ap()
    d_int = nc.dram_tensor("d_int", [IB, 128, TOK], BF16, kind=KIND).ap()
    d_y = nc.dram_tensor("d_y", [TOK, L], F32, kind=KIND).ap()
    d_t_v = nc.dram_tensor("d_t_v", [2, T, N2F, N1], BF16, kind=KIND).ap()
    d_t_Y = nc.dram_tensor("d_t_Y", [2, K1N, 2, N1, T], BF16, kind=KIND).ap()
    d_t_P = nc.dram_tensor("d_t_P", [2, 128, 32, T], BF16, kind=KIND).ap()
    d_t_TA = nc.dram_tensor("d_t_TA", [2, 64, K1N, T], BF16, kind=KIND).ap()
    d_t_Yh = nc.dram_tensor("d_t_Yh", [2, K1N, 2, N1, T], BF16, kind=KIND).ap()
    d_t_h = nc.dram_tensor("d_t_h", [2, T, N2F, N1], BF16, kind=KIND).ap()
    d_t_G = nc.dram_tensor("d_t_G", [2, 128, 2, 4096], BF16, kind=KIND).ap()
    d_t_TG = nc.dram_tensor("d_t_TG", [2, 64, K1N, T], BF16, kind=KIND).ap()

    tabs = _tables()
    c_srcs = [nc.inline_tensor(t, name=f"c_t{i}").ap() for i, t in enumerate(tabs)]

    dbg = {}
    if debug:
        for nm, shp, dt_ in [
            ("dbg_vfft", [128, 4096], BF16), ("dbg_Y", [128, 8192], BF16),
            ("dbg_F", [128, 8192], BF16), ("dbg_P", [128, 4096], BF16),
            ("dbg_corr", [128, 4096], F32), ("dbg_h", [128, 4096], BF16),
            ("dbg_att", [128, 4096], F32), ("dbg_x1", [128, 4096], F32),
            ("dbg_v2t", [128, 4096], BF16), ("dbg_H", [128, 8192], BF16),
            ("dbg_st", [128, 16], F32), ("dbg_sx", [128, 4096], F32),
            ("dbg_bn", [128, 64], F32),
        ]:
            dbg[nm] = nc.dram_tensor(nm, shp, dt_, kind="ExternalOutput").ap()

    EMIT = {k: [] for k in ("tensor", "vector", "scalar", "gpsimd", "sync")}

    def op(eng, build, waits=(), incs=()):
        ws = [(s.h, int(v)) for (s, v) in waits if v is not None]
        ics = [(s.h, int(d)) for (s, d) in incs]
        def f(e, _b=build, _w=ws, _i=ics):
            for h, v in _w:
                e.wait_ge(h, v)
            ins = _b(e)
            if _i:
                if ins is None:
                    ins = e.drain()
                for h, d in _i:
                    ins.then_inc(h, d)
        EMIT[eng].append(f)
        for s, d in incs:
            s.n += d

    sems = {}
    def S(name):
        if name not in sems:
            sems[name] = Sem(ctx, nc, name)
        return sems[name]

    bar = S("bar")
    def barrier():
        bar.n += 5
        tgt = bar.n
        for e in EMIT:
            def f(eng, _t=tgt):
                eng.drain().then_inc(bar.h, 1)
                eng.wait_ge(bar.h, _t)
            EMIT[e].append(f)

    s_yst = ctx.enter_context(nc.sbuf_tensor("s_yst", [128, 8 * 48], F32))
    s_st = ctx.enter_context(nc.sbuf_tensor("s_st", [128, 16], F32))
    s_bn = ctx.enter_context(nc.sbuf_tensor("s_bn", [128, 64], F32))
    s_eps = ctx.enter_context(nc.sbuf_tensor("s_eps", [128, 1], F32))

    MU1, VAR1, IS1 = s_st[:, 0:1], s_st[:, 1:2], s_st[:, 2:3]
    MU2, VAR2, IS2 = s_st[:, 3:4], s_st[:, 4:5], s_st[:, 5:6]
    MU3, VAR3, IS3 = s_st[:, 6:7], s_st[:, 7:8], s_st[:, 8:9]
    NEGM, ZAC, RZ = s_st[:, 9:10], s_st[:, 10:11], s_st[:, 11:12]
    T8TH = s_st[:, 12:13]
    TMP1 = s_st[:, 14:15]

    sm_v = S("sm_v"); sm_a = S("sm_a"); sm_g = S("sm_g"); sm_t = S("sm_t")
    op("vector", lambda e: e.memset(s_eps[:], EPS))
    sm_x = S("sm_x"); sm_d1 = S("sm_d1"); sm_d2 = S("sm_d2"); sm_d3 = S("sm_d3")
    sm_evv = S("sm_evv"); sm_eva = S("sm_eva")

    def agg_stats(e, bnsl, ng, total, mu, var, msq_scr):
        means = bnsl[:, 0:3 * ng].rearrange("p (g k) -> p g k", k=3)[:, :, 1]
        nvars = bnsl[:, 0:3 * ng].rearrange("p (g k) -> p g k", k=3)[:, :, 2]
        e.reduce_sum(mu, means, axis=AXX)
        e.reduce_sum(var, nvars, axis=AXX)
        e.tensor_tensor(msq_scr[:, 0:ng], means, means, ALU.mult)
        e.reduce_sum(TMP1, msq_scr[:, 0:ng], axis=AXX)
        e.tensor_scalar(mu, mu, 1.0 / ng, None, ALU.mult)
        e.tensor_scalar(var, var, 1.0 / total, None, ALU.mult)
        e.tensor_scalar(TMP1, TMP1, 1.0 / ng, None, ALU.mult)
        e.tensor_tensor(var, var, TMP1, ALU.add)
        e.tensor_tensor(msq_scr[:, 0:1], mu, mu, ALU.mult)
        return e.tensor_tensor(var, var, msq_scr[:, 0:1], ALU.subtract)

    def ln_stats(x_ap, n_chunks, mu, var, is_, waits=(), trash=None):
        n = 512 * n_chunks
        SUMC, NEGMU = s_st[:, 13:14], s_st[:, 15:16]
        def a_ops(e, _x=x_ap, _t=trash):
            e.activation(_t, _x, AF.Copy, bias=0.0, scale=1.0, accum_out=SUMC)
            e.mul(TMP1, s_eps[:, 0:1], 1.0)
            return e.mul(TMP1, s_eps[:, 0:1], 1.0)
        op("scalar", a_ops, waits=waits, incs=[(sm_a, 1)])
        def b_ops(e, _x=x_ap, _t=trash):
            e.mul(NEGMU, SUMC, -1.0 / n)
            e.mul(mu, SUMC, 1.0 / n)
            e.mul(TMP1, s_eps[:, 0:1], 1.0)
            e.activation(_t, _x, AF.Square, bias=NEGMU, scale=1.0, accum_out=var)
            e.mul(TMP1, s_eps[:, 0:1], 1.0)
            e.mul(TMP1, s_eps[:, 0:1], 1.0)
            e.activation(is_, var, AF.Ln, bias=s_eps[:, 0:1], scale=1.0 / n)
            e.mul(TMP1, s_eps[:, 0:1], 1.0)
            return e.activation(is_, is_, AF.Exp, bias=0.0, scale=-0.5)
        op("scalar", b_ops, incs=[(sm_a, 1)])
        return sm_a.n

    actx = ExitStack()
    sbA = lambda nm, shp, dt_: actx.enter_context(nc.sbuf_tensor(nm, shp, dt_))
    c_w1 = sbA("c_w1", [128, 8192], BF16)
    c_wb = sbA("c_wb", [128, 8192], BF16)
    c_w2 = sbA("c_w2", [64, 64], BF16)
    c_wa = sbA("c_wa", [128, 64], BF16)
    c_wa2 = sbA("c_wa2", [128, 64], BF16)
    c_id = sbA("c_id", [128, 128], BF16)
    sm_c = S("sm_c")
    for dst, src in zip((c_w1, c_w2, c_wa, c_wb, c_wa2, c_id),
                        (c_srcs[0], c_srcs[1], c_srcs[2], c_srcs[3], c_srcs[4], c_srcs[5])):
        op("sync", lambda e, d=dst, s=src: e.dma_start(d[:], s), incs=[(sm_c, 16)])
    CONST_N = sm_c.n
    s_x = sbA("s_x", [128, 4096], F32)
    s_vfft = sbA("s_vfft", [128, 4096], BF16)
    s_yreg = sbA("s_yreg", [128, 8192], BF16)
    s_big2 = sbA("s_big2", [128, 16384], BF16)
    s_tap = sbA("s_tap", [128, 8192], BF16)
    s_f = sbA("s_f", [128, 8192], BF16)
    s_h2 = sbA("s_h2", [128, 8192], BF16)
    s_p = sbA("s_p", [128, 4096], BF16)
    s_corr = sbA("s_corr", [128, 4096], F32)
    s_e = sbA("s_e", [128, 4096], F32)
    s_hm = sbA("s_hm", [128, 4096], BF16)
    s_v2f = sbA("s_v2f", [128, 1024], BF16)
    s_t8 = sbA("s_t8", [128, 8], F32)
    pA = [actx.enter_context(nc.psum_tensor(f"pA{i}", [128, 512], F32)) for i in range(2)]
    pB_ = [actx.enter_context(nc.psum_tensor(f"pB{i}", [64, 512], F32)) for i in range(2)]
    pT = actx.enter_context(nc.psum_tensor("pT", [128, 1024], BF16))

    pa_free = {0: None, 1: None}
    pb_free = {0: None, 1: None}
    last = {}

    def evac(eng, build, waits):
        sem = sm_evv if eng == "vector" else sm_eva
        op(eng, build, waits=waits, incs=[(sem, 1)])
        return (sem, sem.n)

    def fwd_fft(src_vfft, dst_pack, p2, src_sem_val, pack_free, y_free, dturnY=d_t_Y):
        for jj in range(16):
            def mm4(e, _jj=jj, _src=src_vfft):
                ins = None
                for u in range(4):
                    j = 4 * _jj + u
                    rhs = _src[:].rearrange("p (t n) -> p t n", n=32)[:, :, j % 32]
                    ins = e.matmul(pA[_jj % 2][:, 128 * u:128 * (u + 1)],
                                   c_w1[:, 128 * j:128 * (j + 1)], rhs,
                                   start=True, stop=True)
                return ins
            w = [(sm_d1, src_sem_val)] if jj == 0 else []
            if pa_free[jj % 2] is not None:
                w.append(pa_free[jj % 2])
            op("tensor", mm4, waits=w, incs=[(sm_t, 1)])
            tprog = sm_t.n
            eng = "vector" if jj % 2 == 0 else "scalar"
            def ev(e, _jj=jj):
                if _jj % 2 == 0:
                    return e.tensor_copy(s_yreg[:, 512 * _jj:512 * (_jj + 1)], pA[_jj % 2][:])
                return e.copy(s_yreg[:, 512 * _jj:512 * (_jj + 1)], pA[_jj % 2][:])
            w = [(sm_t, tprog)]
            if jj < 2 and y_free is not None:
                w.append(y_free)
            pa_free[jj % 2] = evac(eng, ev, w)
        s1_done = sm_t.n
        if astop == 11:
            return (sm_eva.n, sm_evv.n), sm_d2.n, s1_done
        def l1(e, _p=p2, _d=dturnY):
            return e.dma_start(_d[_p], s_yreg[:].rearrange("p (e n t) -> p e n t", e=2, n=32))
        op("scalar", l1, waits=[(sm_eva, sm_eva.n), (sm_evv, sm_evv.n)], incs=[(sm_d2, 16)])
        yl1 = sm_d2.n
        def l2(e, _p=p2, _d=dturnY):
            return e.dma_start(s_big2[0:64, :].rearrange("p (k t) -> p k t", k=128),
                               _d[_p].transpose([1, 2, 0, 3]).rearrange("e n k t -> (e n) k t"))
        op("scalar", l2, waits=[(sm_d2, yl1)], incs=[(sm_d2, 16)])
        yl2 = sm_d2.n
        if astop == 12:
            return (sm_eva.n, sm_evv.n), yl1, s1_done
        for ep in range(2):
            for sl in range(8):
                def mmF(e, _ep=ep, _sl=sl):
                    ins = None
                    for g in range(4):
                        e.ldweights(c_w2[:, 32 * _ep:32 * _ep + 32],
                                    tile_position=(0, 32 * g))
                        rhs = s_big2[0:64, 4096 * g + 512 * _sl: 4096 * g + 512 * (_sl + 1)]
                        ins = e.matmul(pA[_sl % 2][32 * g:32 * g + 32, :],
                                       c_w2[:, 32 * _ep:32 * _ep + 32], rhs,
                                       start=True, stop=True, tile_position=(0, 32 * g))
                    return ins
                w = []
                if ep == 0 and sl == 0:
                    w.append((sm_d2, yl2))
                    if pack_free is not None:
                        w.extend(pack_free)
                if pa_free[sl % 2] is not None:
                    w.append(pa_free[sl % 2])
                op("tensor", mmF, waits=w, incs=[(sm_t, 1)])
                tprog = sm_t.n
                def evF(e, _ep=ep, _sl=sl, _d=dst_pack):
                    return e.copy(_d[:, 4096 * _ep + 512 * _sl: 4096 * _ep + 512 * (_sl + 1)],
                                  pA[_sl % 2][:])
                pa_free[sl % 2] = evac("scalar", evF, [(sm_t, tprog)])
        return (sm_eva.n, sm_evv.n), yl1, s1_done

    for tile in range(nt):
        p2 = tile % 2
        w = [(sm_c, CONST_N)] if tile == 0 else [(sm_v, last["x_free"])]
        op("sync", lambda e, _t=tile: e.dma_start(s_x[:], x_in[128 * _t:128 * (_t + 1), :]),
           waits=w, incs=[(sm_x, 16)])
        xw = sm_x.n
        wl = [(sm_x, xw)]
        if tile >= 1:
            wl.append((sm_t, last["v2t_done"]))
        a1 = ln_stats(s_x[:], 8, MU1, VAR1, IS1, waits=wl, trash=s_hm[:])
        w_hm = [(sm_a, a1)]
        if tile >= 1:
            w_hm.append((sm_d1, last["hleg1"]))
        op("vector", lambda e: e.tensor_scalar(s_hm[:], s_x[:], MU1, IS1,
                                               ALU.subtract, ALU.mult),
           waits=w_hm, incs=[(sm_v, 1)])
        vbfn = sm_v.n
        wv1 = [(sm_v, vbfn)]
        if tile >= 1:
            wv1.append((sm_d1, last["vleg2"]))
        op("sync", lambda e, _p=p2: e.dma_start(
            d_t_v[_p], s_hm[:].rearrange("p (n2 n1) -> p n2 n1", n1=32)),
           waits=wv1, incs=[(sm_d1, 16)])
        vl1 = sm_d1.n
        wv2 = [(sm_d1, vl1)]
        if tile >= 1:
            wv2.append((sm_t, last["s1h_done"]))
        op("sync", lambda e, _p=p2: e.dma_start(
            s_vfft[:].rearrange("p (t n) -> p t n", n=32),
            d_t_v[_p].transpose([1, 0, 2])),
           waits=wv2, incs=[(sm_d1, 16)])
        vleg2 = sm_d1.n
        last["vleg2"] = vleg2
        if debug and tile == 0:
            op("gpsimd", lambda e: e.dma_start(dbg["dbg_st"], s_st[:]),
               waits=[(sm_d1, vl1)], incs=[(sm_d3, 16)])
            op("gpsimd", lambda e: e.dma_start(dbg["dbg_sx"], s_x[:]),
               waits=[(sm_d1, vl1)], incs=[(sm_d3, 16)])
            op("gpsimd", lambda e: e.dma_start(dbg["dbg_bn"], s_bn[:]),
               waits=[(sm_d1, vl1)], incs=[(sm_d3, 16)])

        if astop <= 1:
            continue
        pfF = None
        if tile >= 1:
            pfF = [(sm_v, last["G_v"]), (sm_g, last["G_g"])]
        yfree = None
        if tile >= 1:
            yfree = (sm_d3, last["gl1"])
        evF_, yl1_v, s1v_done = fwd_fft(s_vfft, s_f, p2, vleg2, pfF, yfree)
        if debug and tile == 0:
            op("gpsimd", lambda e: e.dma_start(dbg["dbg_vfft"], s_vfft[:]),
               waits=[(sm_d1, vleg2)], incs=[(sm_d3, 16)])
            op("gpsimd", lambda e: e.dma_start(dbg["dbg_F"], s_f[:]),
               waits=[(sm_eva, evF_[0])], incs=[(sm_d3, 16)])

        if astop <= 2 or astop in (11, 12):
            continue
        op("scalar", lambda e: e.activation(s_e[:], s_f[:, 0:4096], AF.Square,
                                            bias=0.0, scale=1.0),
           waits=[(sm_eva, evF_[0])], incs=[(sm_a, 1)])
        a_sq = sm_a.n
        wgp = [(sm_eva, evF_[0])]
        if tile >= 1:
            wgp.append((sm_d3, last["pleg2"]))
        op("gpsimd", lambda e: e.tensor_tensor(s_p[:], s_f[:, 4096:8192],
                                               s_f[:, 4096:8192], ALU.mult),
           waits=wgp, incs=[(sm_g, 1)])
        g_sq = sm_g.n
        op("vector", lambda e: e.tensor_tensor(s_p[:], s_p[:], s_e[:], ALU.add),
           waits=[(sm_a, a_sq), (sm_g, g_sq)], incs=[(sm_v, 1)])
        vP = sm_v.n
        if debug and tile == 0:
            op("gpsimd", lambda e: e.dma_start(dbg["dbg_P"], s_p[:]),
               waits=[(sm_v, vP)], incs=[(sm_d3, 16)])

        wp1 = [(sm_v, vP)]
        if tile >= 1:
            wp1.append((sm_d3, last["pleg2"]))
        op("gpsimd", lambda e, _p=p2: e.dma_start(
            d_t_P[_p], s_p[:].rearrange("p (kl t) -> p kl t", kl=32)),
           waits=wp1, incs=[(sm_d3, 16)])
        pl1 = sm_d3.n
        wp2 = [(sm_d3, pl1)]
        if tile >= 1:
            wp2.append((sm_t, last["gA_done"]))
        op("gpsimd", lambda e, _p=p2: e.dma_start(
            s_big2[64:96, :].rearrange("p (g m) -> p g m", g=4),
            d_t_P[_p].rearrange("(g k) kl t -> g k (kl t)", g=4).transpose([1, 0, 2])),
           waits=wp2, incs=[(sm_d3, 16)])
        pl2 = sm_d3.n
        last["pleg2"] = pl2

        for sl in range(32):
            def mmA(e, _sl=sl):
                return e.matmul(pB_[_sl % 2][:], c_wa[64:96, :],
                                s_big2[64:96, 512 * _sl:512 * (_sl + 1)],
                                start=True, stop=True, tile_position=(64, 0))
            w = [(sm_d3, pl2)] if sl == 0 else []
            if pb_free[sl % 2] is not None:
                w.append(pb_free[sl % 2])
            op("tensor", mmA, waits=w, incs=[(sm_t, 1)])
            tprog = sm_t.n
            def evA(e, _sl=sl):
                return e.tensor_copy(s_big2[0:64, 512 * _sl:512 * (_sl + 1)], pB_[_sl % 2][:])
            pb_free[sl % 2] = evac("vector", evA, [(sm_t, tprog)])
        tA_done = sm_t.n
        evTA = (sm_evv, sm_evv.n)
        op("scalar", lambda e, _p=p2: e.dma_start(
            d_t_TA[_p], s_big2[0:64, :].rearrange("p (k t) -> p k t", k=128)),
           waits=[evTA], incs=[(sm_d2, 16)])
        tal1 = sm_d2.n
        wta = [(sm_d2, tal1)]
        if tile >= 1:
            wta.append((sm_t, last["cB2_done"]))
        op("scalar", lambda e, _p=p2: e.dma_start(
            s_tap[:].rearrange("p (en t) -> p en t", en=64),
            d_t_TA[_p].transpose([1, 0, 2])),
           waits=wta, incs=[(sm_d2, 16)])
        tal2 = sm_d2.n

        for fl in range(8):
            def mmB(e, _fl=fl):
                ins = None
                for u in range(4):
                    n1v = 4 * _fl + u
                    tgt = pA[_fl % 2][:, 128 * u:128 * (u + 1)]
                    e.matmul(tgt, s_tap[:, 128 * n1v:128 * (n1v + 1)],
                             c_wb[:, 128 * n1v:128 * (n1v + 1)], start=True, stop=False)
                    ins = e.matmul(tgt, s_tap[:, 4096 + 128 * n1v:4096 + 128 * (n1v + 1)],
                                   c_wb[:, 4096 + 128 * n1v:4096 + 128 * (n1v + 1)],
                                   start=False, stop=True)
                return ins
            w = [(sm_d2, tal2)] if fl == 0 else []
            if pa_free[fl % 2] is not None:
                w.append(pa_free[fl % 2])
            op("tensor", mmB, waits=w, incs=[(sm_t, 1)])
            tprog = sm_t.n
            def evB(e, _fl=fl):
                dst = s_corr[:].rearrange("p (n2 n1) -> p n2 n1", n1=32)[:, :, 4 * _fl:4 * _fl + 4]
                return e.tensor_copy(dst.transpose([0, 2, 1]),
                                     pA[_fl % 2][:].rearrange("p (a b) -> p a b", a=4))
            w = [(sm_t, tprog)]
            if fl == 0 and tile >= 1:
                w.append((sm_d1, last["x1st"]))
            pa_free[fl % 2] = evac("vector", evB, w)
        evC = (sm_evv, sm_evv.n)
        if debug and tile == 0:
            op("gpsimd", lambda e: e.dma_start(dbg["dbg_corr"], s_corr[:]),
               waits=[evC], incs=[(sm_d3, 16)])

        if astop <= 3:
            continue
        def top8(e):
            e.max(out=s_t8[:], in_=s_corr[:])
            e.tensor_scalar(s_bn[:, 63:64], s_eps[:], 1.0, None, ALU.mult)
            e.tensor_scalar(T8TH, s_t8[:, 7:8], 1.0, None, ALU.mult)
            e.tensor_scalar(NEGM, s_t8[:, 0:1], -1.0, None, ALU.mult)
            return e.tensor_scalar(s_p[:], s_corr[:], T8TH, None, ALU.is_ge)
        op("vector", top8, waits=[evC, (sm_d3, pl2)], incs=[(sm_v, 1)])
        v_t8 = sm_v.n
        op("scalar", lambda e: e.activation(s_e[:], s_corr[:], AF.Exp, bias=NEGM, scale=1.0),
           waits=[(sm_v, v_t8)], incs=[(sm_a, 1)])
        a_exp = sm_a.n
        def hmk(e):
            return e.scalar_tensor_tensor(s_hm[:], s_e[:], 1.0, s_p[:],
                                          ALU.mult, ALU.mult)
        op("vector", hmk, waits=[(sm_a, a_exp), (sm_d1, vl1)], incs=[(sm_v, 1)])
        v_hun = sm_v.n
        def zacc(e):
            e.activation(s_p[:], s_hm[:], AF.Copy, bias=0.0, scale=1.0, accum_out=ZAC)
            e.mul(TMP1, s_eps[:, 0:1], 1.0)
            return e.mul(TMP1, s_eps[:, 0:1], 1.0)
        op("scalar", zacc, waits=[(sm_v, v_hun)], incs=[(sm_a, 1)])
        a_z = sm_a.n
        def hnorm(e):
            e.reciprocal(RZ, ZAC)
            e.tensor_scalar(s_bn[:, 63:64], s_eps[:], 1.0, None, ALU.mult)
            return e.tensor_scalar(s_hm[:], s_hm[:], RZ, None, ALU.mult)
        op("vector", hnorm, waits=[(sm_a, a_z)], incs=[(sm_v, 1)])
        v_h = sm_v.n
        if debug and tile == 0:
            op("gpsimd", lambda e: e.dma_start(dbg["dbg_h"], s_hm[:]),
               waits=[(sm_v, v_h)], incs=[(sm_d3, 16)])

        op("sync", lambda e, _p=p2: e.dma_start(
            d_t_h[_p], s_hm[:].rearrange("p (n2 n1) -> p n2 n1", n1=32)),
           waits=[(sm_v, v_h)], incs=[(sm_d1, 16)])
        hl1 = sm_d1.n
        last["hleg1"] = hl1
        op("sync", lambda e, _p=p2: e.dma_start(
            s_vfft[:].rearrange("p (t n) -> p t n", n=32),
            d_t_h[_p].transpose([1, 0, 2])),
           waits=[(sm_d1, hl1), (sm_t, s1v_done)], incs=[(sm_d1, 16)])
        hleg2 = sm_d1.n

        if astop <= 4:
            continue
        pfH = None
        if tile >= 1:
            pfH = [(sm_v, last["G_v"]), (sm_g, last["G_g"])]
        evH_, yh_l1, s1h_done = fwd_fft(s_vfft, s_h2, p2, hleg2, pfH,
                                        (sm_d2, yl1_v), dturnY=d_t_Yh)
        last["s1h_done"] = s1h_done
        if debug and tile == 0:
            op("gpsimd", lambda e: e.dma_start(dbg["dbg_H"], s_h2[:]),
               waits=[(sm_eva, evH_[0])], incs=[(sm_d3, 16)])

        if astop <= 5:
            continue
        Fre, Fim = s_f[:, 0:4096], s_f[:, 4096:8192]
        Hre, Him = s_h2[:, 0:4096], s_h2[:, 4096:8192]
        Gre, Gim = s_yreg[:, 0:4096], s_yreg[:, 4096:8192]
        op("vector", lambda e: e.tensor_tensor(s_e[:], Fre, Hre, ALU.mult),
           waits=[(sm_eva, evH_[0])], incs=[(sm_v, 1)])
        op("gpsimd", lambda e: e.tensor_tensor(s_p[:], Fim, Him, ALU.mult),
           waits=[(sm_eva, evH_[0])], incs=[(sm_g, 1)])
        op("vector", lambda e: e.tensor_tensor(Gre, s_e[:], s_p[:], ALU.add),
           waits=[(sm_v, sm_v.n), (sm_g, sm_g.n), (sm_d2, yh_l1)], incs=[(sm_v, 1)])
        op("gpsimd", lambda e: e.tensor_tensor(s_p[:], Fim, Hre, ALU.mult),
           waits=[(sm_v, sm_v.n)], incs=[(sm_g, 1)])
        op("vector", lambda e: e.tensor_tensor(s_e[:], Fre, Him, ALU.mult),
           waits=[(sm_v, sm_v.n)], incs=[(sm_v, 1)])
        op("vector", lambda e: e.tensor_tensor(Gim, s_p[:], s_e[:], ALU.subtract),
           waits=[(sm_v, sm_v.n), (sm_g, sm_g.n)], incs=[(sm_v, 1)])
        last["G_v"] = sm_v.n
        last["G_g"] = sm_g.n
        vG = sm_v.n

        op("gpsimd", lambda e, _p=p2: e.dma_start(
            d_t_G[_p], s_yreg[:].rearrange("p (e m) -> p e m", e=2)),
           waits=[(sm_v, vG)], incs=[(sm_d3, 16)])
        gl1 = sm_d3.n
        last["gl1"] = gl1
        for epv in range(2):
            def gl2(e, _p=p2, _e=epv):
                return e.dma_start(
                    s_big2[64 + 32 * _e:96 + 32 * _e, :].rearrange("p (g m) -> p g m", g=4),
                    d_t_G[_p].rearrange("(g k) e m -> g k e m", g=4)[:, :, _e, :]
                    .transpose([1, 0, 2]))
            op("gpsimd", gl2, waits=[(sm_d3, gl1), (sm_t, tA_done)], incs=[(sm_d3, 16)])
        gl2n = sm_d3.n

        for sl in range(32):
            def mmG(e, _sl=sl):
                return e.matmul(pB_[_sl % 2][:], c_wa2[64:128, :],
                                s_big2[64:128, 512 * _sl:512 * (_sl + 1)],
                                start=True, stop=True, tile_position=(64, 0))
            w = [(sm_d3, gl2n)] if sl == 0 else []
            if pb_free[sl % 2] is not None:
                w.append(pb_free[sl % 2])
            op("tensor", mmG, waits=w, incs=[(sm_t, 1)])
            tprog = sm_t.n
            def evG(e, _sl=sl):
                return e.tensor_copy(s_big2[0:64, 512 * _sl:512 * (_sl + 1)], pB_[_sl % 2][:])
            pb_free[sl % 2] = evac("vector", evG, [(sm_t, tprog)])
        gA_done = sm_t.n
        last["gA_done"] = gA_done
        evTG = (sm_evv, sm_evv.n)
        op("gpsimd", lambda e, _p=p2: e.dma_start(
            d_t_TG[_p], s_big2[0:64, :].rearrange("p (k t) -> p k t", k=128)),
           waits=[evTG], incs=[(sm_d3, 16)])
        tgl1 = sm_d3.n
        op("gpsimd", lambda e, _p=p2: e.dma_start(
            s_tap[:].rearrange("p (en t) -> p en t", en=64),
            d_t_TG[_p].transpose([1, 0, 2])),
           waits=[(sm_d3, tgl1)], incs=[(sm_d3, 16)])
        tgl2 = sm_d3.n

        for fl in range(8):
            def mmB2(e, _fl=fl):
                ins = None
                for u in range(4):
                    n1v = 4 * _fl + u
                    tgt = pA[_fl % 2][:, 128 * u:128 * (u + 1)]
                    e.matmul(tgt, s_tap[:, 128 * n1v:128 * (n1v + 1)],
                             c_wb[:, 128 * n1v:128 * (n1v + 1)], start=True, stop=False)
                    ins = e.matmul(tgt, s_tap[:, 4096 + 128 * n1v:4096 + 128 * (n1v + 1)],
                                   c_wb[:, 4096 + 128 * n1v:4096 + 128 * (n1v + 1)],
                                   start=False, stop=True)
                return ins
            w = [(sm_d3, tgl2)] if fl == 0 else []
            if pa_free[fl % 2] is not None:
                w.append(pa_free[fl % 2])
            op("tensor", mmB2, waits=w, incs=[(sm_t, 1)])
            tprog = sm_t.n
            def evB2(e, _fl=fl):
                dst = s_e[:].rearrange("p (n2 n1) -> p n2 n1", n1=32)[:, :, 4 * _fl:4 * _fl + 4]
                return e.tensor_copy(dst.transpose([0, 2, 1]),
                                     pA[_fl % 2][:].rearrange("p (a b) -> p a b", a=4))
            pa_free[fl % 2] = evac("vector", evB2, [(sm_t, tprog)])
        last["cB2_done"] = sm_t.n
        evAtt = (sm_evv, sm_evv.n)
        if debug and tile == 0:
            op("gpsimd", lambda e: e.dma_start(dbg["dbg_att"], s_e[:]),
               waits=[evAtt], incs=[(sm_d3, 16)])

        if astop <= 6:
            continue
        a2 = ln_stats(s_e[:], 8, MU2, VAR2, IS2, waits=[evAtt], trash=s_hm[:])
        def x1c(e):
            e.tensor_scalar(s_corr[:], s_e[:], MU2, IS2, ALU.subtract, ALU.mult)
            return e.tensor_tensor(s_corr[:], s_corr[:], s_x[:], ALU.add)
        op("vector", x1c, waits=[(sm_a, a2)], incs=[(sm_v, 1)])
        last["x_free"] = sm_v.n
        v_x1 = sm_v.n
        op("sync", lambda e, _t=tile: e.dma_start(d_x1[128 * _t:128 * (_t + 1), :], s_corr[:]),
           waits=[(sm_v, v_x1)], incs=[(sm_d1, 16)])
        last["x1st"] = sm_d1.n
        if debug and tile == 0:
            op("gpsimd", lambda e: e.dma_start(dbg["dbg_x1"], s_corr[:]),
               waits=[(sm_v, v_x1)], incs=[(sm_d3, 16)])

        a3 = ln_stats(s_corr[:], 8, MU3, VAR3, IS3,
                      waits=[(sm_v, v_x1), (sm_d1, hl1)], trash=s_hm[:])
        op("vector", lambda e: e.tensor_scalar(s_hm[:], s_corr[:], MU3, IS3,
                                               ALU.subtract, ALU.mult),
           waits=[(sm_a, a3)], incs=[(sm_v, 1)])
        v_v2t = sm_v.n
        if debug and tile == 0:
            op("gpsimd", lambda e: e.dma_start(dbg["dbg_v2t"], s_hm[:]),
               waits=[(sm_v, v_v2t)], incs=[(sm_d3, 16)])

        if astop <= 7:
            continue
        for qb in range(4):
            def tps(e, _qb=qb):
                ins = None
                for q8 in range(8):
                    q = 8 * _qb + q8
                    ins = e.transpose(pT[:, 128 * q8:128 * (q8 + 1)],
                                      s_hm[:, 128 * q:128 * (q + 1)], c_id[:])
                return ins
            w = [(sm_v, v_v2t)]
            if qb >= 1:
                w.append((sm_a, sm_a.n))
            op("tensor", tps, waits=w, incs=[(sm_t, 1)])
            tprog = sm_t.n
            w = [(sm_t, tprog)]
            if qb >= 1:
                w.append((sm_d3, sm_d3.n))
            op("scalar", lambda e: e.copy(s_v2f[:], pT[:]), waits=w, incs=[(sm_a, 1)])
            aprog = sm_a.n
            def v2st(e, _qb=qb, _t=tile):
                return e.dma_start(d_v2[8 * _qb:8 * (_qb + 1), :, 128 * _t:128 * (_t + 1)]
                            .transpose([1, 0, 2]),
                            s_v2f[:].rearrange("p (q t) -> p q t", q=8))
            op("gpsimd", v2st, waits=[(sm_a, aprog)], incs=[(sm_d3, 16)])
        last["v2t_done"] = sm_t.n

    A_V2 = sm_d3.n
    A_X1 = sm_d1.n
    actx.close()
    barrier()

    if not do_ffn:
        with nc.Block() as blk:
            blk.tensor(lambda t: [f(t) for f in EMIT["tensor"]])
            blk.vector(lambda v: [f(v) for f in EMIT["vector"]])
            blk.scalar(lambda s: [f(s) for f in EMIT["scalar"]])
            blk.gpsimd(lambda g: [f(g) for f in EMIT["gpsimd"]])
            blk.sync(lambda s: [f(s) for f in EMIT["sync"]])
        _CACHE[key] = nc
        return nc

    bctx = ExitStack()
    sbB = lambda nm, shp, dt_: bctx.enter_context(nc.sbuf_tensor(nm, shp, dt_))
    s_v2 = sbB("s_v2", [128, NQ * TOK], BF16)
    s_wgb = [sbB(f"s_wgb{i}", [128, 4096], BF16) for i in range(2)]
    s_wub = [sbB(f"s_wub{i}", [128, 4096], BF16) for i in range(2)]
    s_sg = sbB("s_sg", [128, 512], F32)
    s_ic = [sbB(f"s_ic{i}", [128, 1024], BF16) for i in range(2)]
    pG = [bctx.enter_context(nc.psum_tensor(f"pG{i}", [128, 512], F32)) for i in range(8)]

    sm_wl = S("sm_wl"); sm_bt = S("sm_bt"); sm_bs = S("sm_bs")
    sm_bm = S("sm_bm"); sm_bi = S("sm_bi"); sm_v2l = S("sm_v2l")

    op("sync", lambda e: e.dma_start(
        s_v2[:].rearrange("p (q t) -> p q t", q=NQ), d_v2.transpose([1, 0, 2])),
       waits=[(sm_d3, A_V2)], incs=[(sm_v2l, 16)])

    mm_done = {}; sil_done = {}; mul_done = {}
    for ib in range(IB):
        pb = ib % 2
        wld = [] if ib < 2 else [(sm_bt, mm_done[ib - 2])]
        op("gpsimd", lambda e, _ib=ib, _pb=pb: e.dma_start(
            s_wgb[_pb][:].rearrange("p (q i) -> p q i", q=NQ),
            wg_in[_ib].transpose([1, 0, 2])), waits=wld, incs=[(sm_wl, 16)])
        op("gpsimd", lambda e, _ib=ib, _pb=pb: e.dma_start(
            s_wub[_pb][:].rearrange("p (q i) -> p q i", q=NQ),
            wu_in[_ib].transpose([1, 0, 2])), incs=[(sm_wl, 16)])
        wn = sm_wl.n
        for gu in range(2):
            for th in range(2):
                bank = 4 * pb + 2 * gu + th
                def mmBk(e, _pb=pb, _gu=gu, _th=th, _bank=bank):
                    wsb = (s_wgb if _gu == 0 else s_wub)[_pb]
                    ins = None
                    for q in range(NQ):
                        ins = e.matmul(pG[_bank][:], wsb[:, 128 * q:128 * (q + 1)],
                                       s_v2[:, TOK * q + 512 * _th: TOK * q + 512 * (_th + 1)],
                                       start=(q == 0), stop=(q == NQ - 1))
                    return ins
                w = [(sm_wl, wn)]
                if ib == 0 and gu == 0 and th == 0:
                    w.append((sm_v2l, 16))
                if ib >= 2:
                    w.append((sm_bs, sil_done[ib - 2]))
                    w.append((sm_bm, mul_done[ib - 2]))
                op("tensor", mmBk, waits=w, incs=[(sm_bt, 1)])
        mm_done[ib] = sm_bt.n
        for th in range(2):
            gb = 4 * pb + th
            ub = 4 * pb + 2 + th
            w = [(sm_bt, mm_done[ib])]
            if th == 1:
                w.append((sm_bm, mul_done[(ib, 0)]))
            elif ib >= 1:
                w.append((sm_bm, mul_done[(ib - 1, 1)]))
            op("scalar", lambda e, _gb=gb: e.activation(s_sg[:], pG[_gb][:], AF.Silu,
                                                        bias=0.0, scale=1.0),
               waits=w, incs=[(sm_bs, 1)])
            asn = sm_bs.n
            w = [(sm_bs, asn)]
            if ib >= 2 and th == 0:
                w.append((sm_bi, 16 * (ib - 1)))
            def mul(e, _ub=ub, _pb=pb, _th=th):
                return e.tensor_tensor(s_ic[_pb][:, 512 * _th:512 * (_th + 1)],
                                       s_sg[:], pG[_ub][:], ALU.mult)
            op("vector", mul, waits=w, incs=[(sm_bm, 1)])
            mul_done[(ib, th)] = sm_bm.n
        sil_done[ib] = sm_bs.n
        mul_done[ib] = sm_bm.n
        op("sync", lambda e, _ib=ib, _pb=pb: e.dma_start(d_int[_ib], s_ic[_pb][:]),
           waits=[(sm_bm, mul_done[ib])], incs=[(sm_bi, 16)])

    bctx.close()
    barrier()

    GRP = 8
    grps = [list(range(g, min(g + GRP, IB))) for g in range(0, IB, GRP)]
    cctx = ExitStack()
    sbC = lambda nm, shp, dt_: cctx.enter_context(nc.sbuf_tensor(nm, shp, dt_))
    s_int = sbC("s_int", [128, IB * TOK], BF16)
    s_wdb = [sbC(f"s_wdb{i}", [128, GRP * 512], BF16) for i in range(2)]
    s_yb = [sbC(f"s_yb{i}", [128, 512], F32) for i in range(2)]
    pC = [cctx.enter_context(nc.psum_tensor(f"pC{i}", [128, 512], F32)) for i in range(8)]

    sm_il = S("sm_il"); sm_cw = S("sm_cw"); sm_ct = S("sm_ct")
    sm_cv = S("sm_cv"); sm_cy = S("sm_cy")

    op("sync", lambda e: e.dma_start(
        s_int[:].rearrange("p (ib t) -> p ib t", ib=IB), d_int.transpose([1, 0, 2])),
       waits=[(sm_bi, 16 * IB)], incs=[(sm_il, 16)])

    grp_done = {}; evh_done = {}
    for hs in range(HS):
        for gi, grp in enumerate(grps):
            pb = gi % 2
            w = []
            ngr = len(grps)
            if gi >= 2:
                prev = (hs, gi - 2)
            elif hs > 0:
                j = ngr - 1 if (ngr - 1) % 2 == gi % 2 else ngr - 2
                prev = (hs - 1, j)
            else:
                prev = None
            if prev is not None and prev in grp_done:
                w.append((sm_ct, grp_done[prev]))
            def wdl(e, _hs=hs, _g0=grp[0], _n=len(grp), _pb=pb):
                return e.dma_start(s_wdb[_pb][:, 0:512 * _n].rearrange("p (b f) -> p b f", b=_n),
                            wd_in[_hs, _g0:_g0 + _n].transpose([1, 0, 2]))
            op("gpsimd", wdl, waits=w, incs=[(sm_cw, 16)])
            cwn = sm_cw.n
            w = [(sm_cw, cwn)]
            if gi == 0:
                if hs == 0:
                    w.append((sm_il, 16))
                else:
                    w.append((sm_cv, evh_done[hs - 1]))
            def mmC(e, _grp=tuple(grp), _pb=pb):
                ins = None
                for bi, ibv in enumerate(_grp):
                    for tg in range(8):
                        ins = e.matmul(pC[tg][:],
                                       s_int[:, TOK * ibv + 128 * tg: TOK * ibv + 128 * (tg + 1)],
                                       s_wdb[_pb][:, 512 * bi:512 * (bi + 1)],
                                       start=(ibv == 0), stop=(ibv == IB - 1),
                                       skip_group_check=True)
                return ins
            op("tensor", mmC, waits=w, incs=[(sm_ct, 1)])
            grp_done[(hs, gi)] = sm_ct.n
        ctn = sm_ct.n
        for tg in range(8):
            pb = tg % 2
            w = [(sm_ct, ctn)]
            if tg >= 2:
                w.append((sm_cy, sm_cy.n))
            def evCy(e, _tg=tg, _pb=pb, _hs=hs):
                return e.tensor_copy(s_yb[_pb][:], pC[_tg][:])
            op("vector", evCy, waits=w, incs=[(sm_cv, 1)])
            cvn = sm_cv.n
            op("sync", lambda e, _tg=tg, _pb=pb, _hs=hs: e.dma_start(
                d_y[128 * _tg:128 * (_tg + 1), 512 * _hs:512 * (_hs + 1)], s_yb[_pb][:]),
               waits=[(sm_cv, cvn)], incs=[(sm_cy, 16)])
        evh_done[hs] = sm_cv.n

    cctx.close()
    barrier()

    dctx = ExitStack()
    sbD = lambda nm, shp, dt_: dctx.enter_context(nc.sbuf_tensor(nm, shp, dt_))
    s_dx = [sbD(f"s_dx{i}", [128, 4096], F32) for i in range(2)]
    s_dy = [sbD(f"s_dy{i}", [128, 4096], F32) for i in range(2)]
    s_dtr = sbD("s_dtr", [128, 4096], BF16)
    MU4, VAR4, IS4 = s_st[:, 0:1], s_st[:, 1:2], s_st[:, 2:3]

    sm_dl = S("sm_dl"); sm_dv = S("sm_dv"); sm_da = S("sm_da"); sm_out = S("sm_out")

    C_Y = sems["sm_cy"].n
    dv_done = {}
    for tg in range(8):
        pb = tg % 2
        w = [] if tg < 2 else [(sm_dv, dv_done[tg - 2])]
        if tg == 0:
            w = w + [(sm_d1, A_X1), (S("sm_cy"), C_Y)]
        op("sync", lambda e, _tg=tg, _pb=pb: e.dma_start(
            s_dx[_pb][:], d_x1[128 * _tg:128 * (_tg + 1), :]), waits=w, incs=[(sm_dl, 16)])
        op("sync", lambda e, _tg=tg, _pb=pb: e.dma_start(
            s_dy[_pb][:], d_y[128 * _tg:128 * (_tg + 1), :]), incs=[(sm_dl, 16)])
        dln = sm_dl.n
        w = [(sm_dl, dln)] if tg < 1 else [(sm_dl, dln), (sm_dv, dv_done[tg - 1])]
        an = ln_stats(s_dy[pb][:], 8, MU4, VAR4, IS4, waits=w, trash=s_dtr[:])
        def fin(e, _pb=pb):
            e.tensor_scalar(s_dy[_pb][:], s_dy[_pb][:], MU4, IS4, ALU.subtract, ALU.mult)
            return e.tensor_tensor(s_dy[_pb][:], s_dy[_pb][:], s_dx[_pb][:], ALU.add)
        op("vector", fin, waits=[(sm_a, an)], incs=[(sm_dv, 1)])
        dv_done[tg] = sm_dv.n
        op("sync", lambda e, _tg=tg, _pb=pb: e.dma_start(
            out_t[128 * _tg:128 * (_tg + 1), :], s_dy[_pb][:]),
           waits=[(sm_dv, dv_done[tg])], incs=[(sm_out, 16)])

    op("sync", lambda e: e.nop(), waits=[(sm_out, sm_out.n)])
    dctx.close()

    with nc.Block() as blk:
        blk.tensor(lambda t: [f(t) for f in EMIT["tensor"]])
        blk.vector(lambda v: [f(v) for f in EMIT["vector"]])
        blk.scalar(lambda s: [f(s) for f in EMIT["scalar"]])
        blk.gpsimd(lambda g: [f(g) for f in EMIT["gpsimd"]])
        blk.sync(lambda s: [f(s) for f in EMIT["sync"]])

    _CACHE[key] = nc
    return nc


def _prep_inputs(L0_attn_x, L0_ffn_w_gate, L0_ffn_w_up, L0_ffn_w_down):
    x = np.ascontiguousarray(np.asarray(L0_attn_x, dtype=np.float32).reshape(8192, L))
    wg = np.ascontiguousarray(
        np.asarray(L0_ffn_w_gate, dtype=np.float32).reshape(IB, 128, NQ, 128)
        .transpose(0, 2, 3, 1))
    wu = np.ascontiguousarray(
        np.asarray(L0_ffn_w_up, dtype=np.float32).reshape(IB, 128, NQ, 128)
        .transpose(0, 2, 3, 1))
    wd = np.ascontiguousarray(
        np.asarray(L0_ffn_w_down, dtype=np.float32).reshape(HS, 512, IB, 128)
        .transpose(0, 2, 3, 1))
    return x, wg, wu, wd


def kernel(L0_attn_x, L0_ffn_w_gate, L0_ffn_w_up, L0_ffn_w_down):
    x, wg, wu, wd = _prep_inputs(L0_attn_x, L0_ffn_w_gate, L0_ffn_w_up, L0_ffn_w_down)
    nc = build_program(debug=False)
    in_maps = [{"x": x[c * TOK:(c + 1) * TOK], "wg": wg, "wu": wu, "wd": wd}
               for c in range(8)]
    res = run_bass_kernel_spmd(nc, in_maps, core_ids=list(range(8)))
    out = np.concatenate([res.results[c]["out"] for c in range(8)], axis=0)
    return out.reshape(1, 8192, L).astype(np.float32)
